# version 15
# speedup vs baseline: 1.1763x; 1.0095x over previous
import numpy as np
import ml_dtypes
from contextlib import ExitStack

import concourse.bass as bass
import concourse.mybir as mybir
from concourse.bass_utils import run_bass_kernel_spmd

F32 = mybir.dt.float32
BF16 = mybir.dt.bfloat16
AF = mybir.ActivationFunctionType
ALU = mybir.AluOpType
AX = mybir.AxisListType

D = 1024
DH = 4096
DI = 2048
DSTATE = 16
DTRANK = 64
NHEADS = 16
HDIM = 64
EPS = 1e-5
WIN = 512

ENGS = ("pe", "act", "dve", "pool", "sp")


class Sem:
    __slots__ = ("h", "val", "name")
    def __init__(self, h, name):
        self.h, self.val, self.name = h, 0, name


class Bank:
    __slots__ = ("t", "freed", "written")
    def __init__(self, t):
        self.t, self.freed, self.written = t, None, None


class Slot:
    __slots__ = ("sem", "freed", "ready")
    def __init__(self, sem):
        self.sem, self.freed, self.ready = sem, None, None


class P:
    def __init__(self, nc, ctx, sim_safe=False):
        self.nc, self.ctx, self.wcache = nc, ctx, {}
        self.sim_safe = sim_safe
        self.es = {e: self.sem("s_" + e) for e in ("pe", "act", "dve", "pool")}
        self.own = {self.es[e].name: e for e in self.es}
        self.dma_sems = []
        self.tok = {}

    def eng(self, e):
        return {"pe": self.nc.tensor, "act": self.nc.scalar, "dve": self.nc.vector,
                "pool": self.nc.gpsimd, "sp": self.nc.sync}[e]

    def sem(self, name):
        return Sem(self.ctx.enter_context(self.nc.semaphore(name)), name)

    def now(self, e):
        s = self.es[e]
        return (s, s.val)

    def inc(self, inst, e, n=1):
        s = self.es[e]
        inst.then_inc(s.h, n)
        s.val += n
        return (s, s.val)

    def wait(self, e, token):
        if token is None:
            return
        if not isinstance(token[0], Sem):
            for t in token:
                self.wait(e, t)
            return
        s, v = token
        if v <= 0:
            return
        if not self.sim_safe and self.own.get(s.name) == e:
            return
        key = (e, s.name)
        if self.wcache.get(key, 0) >= v:
            return
        self.eng(e).wait_ge(s.h, v)
        self.wcache[key] = v

    def waits(self, e, tokens):
        for t in tokens:
            self.wait(e, t)

    def slot(self, name):
        s = Slot(self.sem(name))
        self.dma_sems.append(s.sem)
        return s

    def dma(self, qe, dst, src, slot, waits=()):
        self.wait(qe, slot.freed)
        self.waits(qe, waits)
        inst = self.eng(qe).dma_start(dst, src)
        inst.then_inc(slot.sem.h, 16)
        slot.sem.val += 16
        slot.ready = (slot.sem, slot.sem.val)
        return slot.ready

    def ve(self, deps, f, *a, **kw):
        self.waits("dve", deps)
        return self.inc(f(*a, **kw), "dve")

    def ae(self, deps, f, *a, **kw):
        self.waits("act", deps)
        return self.inc(f(*a, **kw), "act")

    def pool_op(self, deps, f, *a, **kw):
        self.waits("pool", deps)
        return self.inc(f(*a, **kw), "pool")

    def mm(self, items, waits=(), sgc=False):
        self.waits("pe", waits)
        inst = None
        for (o, l, r, st, sp) in items:
            inst = self.nc.tensor.matmul(o, l, r, start=st, stop=sp,
                                         skip_group_check=sgc)
        return self.inc(inst, "pe")

    def barrier(self):
        toks = [(s, s.val) for s in self.es.values()]
        toks += [(s, s.val) for s in self.dma_sems]
        for e in ENGS:
            for t in toks:
                self.wait(e, t)


def ap3(t, offset, dims):
    return bass.AP(t, offset, dims)


class WStream:

    def __init__(self, p, cache_t, KT, OT, bufs, slots, rdy=None):
        self.p, self.c, self.KT, self.OT = p, cache_t, KT, OT
        self.bufs, self.slots, self.rdy = bufs, slots, rdy
        self.nb = len(bufs)
        if KT <= 16:
            self.opp = max(1, 2048 // (KT * 128))
            self.npan = (OT + self.opp - 1) // self.opp
            self.half = False
        else:
            self.opp = 1
            self.npan = OT * 2
            self.half = True
        self.issued = -1

    def _issue(self, pi):
        if pi >= self.npan or pi <= self.issued:
            return
        b, s = self.bufs[pi % self.nb], self.slots[pi % self.nb]
        w = (self.rdy,) if self.rdy else ()
        if self.half:
            j, h = pi // 2, pi % 2
            src = ap3(self.c, j * 128 * self.KT * 128 + h * 16 * 128,
                      [[self.KT * 128, 128], [1, 16 * 128]])
            self.p.dma("sp", b[:, 0:2048], src, s, waits=w)
        else:
            j0 = pi * self.opp
            no = min(self.opp, self.OT - j0)
            k128 = self.KT * 128
            src = ap3(self.c, j0 * 128 * k128,
                      [[k128, 128], [128 * k128, no], [1, k128]])
            self.p.dma("sp",
                       b[:, 0:no * k128].rearrange("p (o k) -> p o k", o=no),
                       src, s, waits=w)
        self.issued = pi

    def _pan(self, j, kt):
        return (j * 2 + kt // 16) if self.half else (j // self.opp)

    def get(self, j, kt):
        pi = self._pan(j, kt)
        self._issue(pi)
        depth = self.nb - 2 if self.half else self.nb - 1
        for q in range(1, depth + 1):
            self._issue(pi + q)
        b, s = self.bufs[pi % self.nb], self.slots[pi % self.nb]
        if self.half:
            col = (kt % 16) * 128
        else:
            col = (j - (pi * self.opp) * 1) * self.KT * 128 + kt * 128
            col = (j - pi * self.opp) * self.KT * 128 + kt * 128
        return b, col, s.ready

    def done(self, j, tok):
        pis = {j * 2, j * 2 + 1} if self.half else {j // self.opp}
        for pi in pis:
            self.slots[pi % self.nb].freed = tok



def emit_ln(p, sh, x_sb, x_tok, xn_sb, x_off, xn_off, x_stride, xn_stride,
            xn_war=None):
    nc = p.nc
    W = 512
    stat, bc_mean, bc_istd = sh["ln_banks"]
    sq, rows, tmp = sh["sq"], sh["rows"], sh["tmp"]
    p.wait("pe", stat.freed)
    p.wait("pe", x_tok)
    p.wait("pe", p.tok.get("const_rdy"))
    for kt in range(8):
        xsl = x_sb[:, kt * x_stride + x_off: kt * x_stride + x_off + W]
        t = p.mm([(stat.t[:], sh["onescol"][:], xsl, kt == 0, kt == 7)], sgc=True)
    stat.written = t
    t_mean = p.ve((t, p.tok.get("ln_rows_war")), nc.vector.tensor_scalar,
                  rows[0:1, 0:W], stat.t[:], 1.0 / D, None, ALU.mult)
    stat.freed = t_mean
    for kt in range(8):
        xsl = x_sb[:, kt * x_stride + x_off: kt * x_stride + x_off + W]
        sl = kt % 2
        atok = p.ae((x_tok, p.tok.get(("sqpe", sl)), p.tok.get(("sqwar", sl))),
                    nc.scalar.activation, sq[sl][:], xsl, AF.Square)
        t = p.mm([(stat.t[:], sh["onescol"][:], sq[sl][:], kt == 0, kt == 7)],
                 waits=(atok, stat.freed), sgc=True)
        p.tok[("sqpe", sl)] = t
    stat.written = t
    t_ex2 = p.ve((t,), nc.vector.tensor_scalar,
                 rows[0:1, 512:512 + W], stat.t[:], 1.0 / D, None, ALU.mult)
    stat.freed = t_ex2
    t_msq = p.ve((t_mean, p.tok.get("ln_tmp_war")), nc.vector.tensor_mul,
                 tmp[0:1, 0:W], rows[0:1, 0:W], rows[0:1, 0:W])
    t_ve = p.ve((t_ex2,), nc.vector.tensor_scalar, rows[0:1, 512:512 + W],
                rows[0:1, 512:512 + W], 1.0, float(EPS), ALU.mult, ALU.add)
    t_var = p.ve((t_msq, t_ve), nc.vector.tensor_sub, rows[0:1, 512:512 + W],
                 rows[0:1, 512:512 + W], tmp[0:1, 0:W])
    t_ln = p.ae((t_var,), nc.scalar.activation, rows[0:1, 512:512 + W],
                rows[0:1, 512:512 + W], AF.Ln)
    t_istd = p.ae((t_ln,), nc.scalar.activation, rows[0:1, 512:512 + W],
                  rows[0:1, 512:512 + W], AF.Exp, scale=-0.5)
    t = p.mm([(bc_mean.t[:], sh["onesrow"][:], rows[0:1, 0:W], True, True)],
             waits=(t_mean, bc_mean.freed))
    bc_mean.written = t
    t2 = p.mm([(bc_istd.t[:], sh["onesrow"][:], rows[0:1, 512:512 + W], True, True)],
              waits=(t_istd, bc_istd.freed))
    bc_istd.written = t2
    p.tok["ln_rows_war"] = t2
    prev = p.tok.get("ln_tmp_war")
    xw = xn_war
    for kt in range(8):
        xsl = x_sb[:, kt * x_stride + x_off: kt * x_stride + x_off + W]
        t_sub = p.ve((t2, xw, prev, t_msq), nc.vector.tensor_sub,
                     tmp[:, 0:W], xsl, bc_mean.t[:])
        prev = p.ve((t_sub,), nc.vector.tensor_mul,
                    xn_sb[:, kt * xn_stride + xn_off: kt * xn_stride + xn_off + W],
                    tmp[:, 0:W], bc_istd.t[:])
        xw = None
    p.tok["ln_tmp_war"] = prev
    bc_mean.freed = bc_istd.freed = prev
    return prev


def make_ln_shared(p, ctx, tag, consts, sq, bc=None):
    nc = p.nc
    sb = lambda n, shp: ctx.enter_context(nc.sbuf_tensor(tag + n, shp, F32))
    ps = lambda n, shp: ctx.enter_context(nc.psum_tensor(tag + n, shp, F32))
    if bc is None:
        bc = (Bank(ps("pbcA", [128, 512])), Bank(ps("pbcB", [128, 512])))
    return {
        "sq": sq,
        "rows": sb("rows", [1, 1024]),
        "tmp": sb("tmp", [128, 512]),
        "onescol": consts["onescol"], "onesrow": consts["onesrow"],
        "ln_banks": (Bank(ps("pstA", [1, 512])), bc[0], bc[1]),
    }



def emit_ffn(p, S, src_t, dst_t, caches, cols, consts, tag):
    nc = p.nc
    cw1, cw2, ch1, ch2, cw3 = caches
    s1c, s2c, sh1c, sh2c, s3c = cols
    FTC = 1024
    NTB = FTC // 512
    nchunks = S // FTC
    with ExitStack() as ctx:
        sb = lambda n, shp, dt: ctx.enter_context(nc.sbuf_tensor(tag + n, shp, dt))
        ps = lambda n, shp: ctx.enter_context(nc.psum_tensor(tag + n, shp, F32))
        g = sb("g", [128, 32 * FTC], BF16)
        o1 = sb("o1", [128, 32 * FTC], BF16)
        xn = sb("xn", [128, 8 * FTC], BF16)
        x_res = sb("x", [128, 8 * FTC], F32)
        pan = [sb(f"pan{i}", [128, 2048], BF16) for i in range(4)]
        g1t = [sb(f"g1t{i}", [128, 512], F32) for i in range(2)]
        banks = [Bank(ps(f"pb{i}", [128, 512])) for i in range(4)]
        sh = make_ln_shared(p, ctx, tag + "L", consts, g1t)
        xld = p.slot(tag + "xld")
        xst = p.slot(tag + "xst")
        wsl = [p.slot(tag + f"w{i}") for i in range(4)]
        bank_i = [0]

        def bank():
            b = banks[bank_i[0] % 4]
            bank_i[0] += 1
            return b

        vtok = atok = stok = None
        for c in range(nchunks):
            c0 = c * FTC
            x_src = ap3(src_t, c0, [[S, 128], [128 * S, 8], [1, FTC]])
            p.dma("sp", x_res[:].rearrange("p (k t) -> p k t", k=8), x_src, xld)
            ln = None
            for tb in range(NTB):
                ln = emit_ln(p, sh, x_res, xld.ready, xn, tb * 512, tb * 512,
                             FTC, FTC, xn_war=p.tok.get(tag + "xn_war"))
            st_w1 = WStream(p, cw1, 8, 32, [pan[0], pan[1]], [wsl[0], wsl[1]])
            st_w2 = WStream(p, cw2, 8, 32, [pan[2], pan[3]], [wsl[2], wsl[3]])
            g1_tok = [None, None]
            for j in range(32):
                items1 = []
                items2 = []
                rws = []
                for kt in range(8):
                    b1, c1, r1 = st_w1.get(j, kt)
                    b2, c2, r2 = st_w2.get(j, kt)
                    items1.append((b1, c1))
                    items2.append((b2, c2))
                    rws += [r1, r2]
                rws = tuple(set(rws))
                for tb in range(NTB):
                    bk1 = bank()
                    t1 = p.mm([(bk1.t[:], items1[kt][0][:, items1[kt][1]:items1[kt][1] + 128],
                                xn[:, kt * FTC + tb * 512: kt * FTC + tb * 512 + 512],
                                kt == 0, kt == 7) for kt in range(8)],
                              waits=(ln, bk1.freed) + rws)
                    bk1.written = t1
                    sl = (j * NTB + tb) % 2
                    p.wait("act", t1)
                    p.wait("act", g1_tok[sl])
                    p.wait("act", p.tok.get(("sqpe", sl)))
                    a = nc.scalar.activation(g1t[sl][:], bk1.t[:], AF.Silu,
                                             scale=s1c[:, j:j + 1])
                    atok = p.inc(a, "act")
                    bk1.freed = atok
                    bk2 = bank()
                    t2 = p.mm([(bk2.t[:], items2[kt][0][:, items2[kt][1]:items2[kt][1] + 128],
                                xn[:, kt * FTC + tb * 512: kt * FTC + tb * 512 + 512],
                                kt == 0, kt == 7) for kt in range(8)],
                              waits=(bk2.freed,))
                    bk2.written = t2
                    p.wait("dve", t2)
                    p.wait("dve", atok)
                    if j == 0 and tb == 0:
                        p.wait("dve", p.tok.get(tag + "g_war"))
                    v = nc.vector.scalar_tensor_tensor(
                        g[:, j * FTC + tb * 512: j * FTC + tb * 512 + 512],
                        bk2.t[:], s2c[:, j:j + 1], g1t[sl][:], ALU.mult, ALU.mult)
                    vtok = p.inc(v, "dve")
                    bk2.freed = vtok
                    g1_tok[sl] = vtok
                    p.tok[("sqwar", sl)] = vtok
                pe_now = p.now("pe")
                st_w1.done(j, pe_now)
                st_w2.done(j, pe_now)
            g_ready = vtok
            p.tok[tag + "xn_war"] = pe_now
            st_h = WStream(p, ch1, 32, 32, pan, wsl)
            for j in range(32):
                hits = []
                rws = []
                for kt in range(32):
                    b_, c_, r_ = st_h.get(j, kt)
                    hits.append((b_, c_))
                    rws.append(r_)
                rws = tuple(set(rws))
                for tb in range(NTB):
                    bk = bank()
                    t = p.mm([(bk.t[:], hits[kt][0][:, hits[kt][1]:hits[kt][1] + 128],
                               g[:, kt * FTC + tb * 512: kt * FTC + tb * 512 + 512],
                               kt == 0, kt == 31) for kt in range(32)],
                             waits=(g_ready, bk.freed) + rws)
                    bk.written = t
                    p.wait("act", t)
                    if j == 0 and tb == 0:
                        p.wait("act", p.tok.get(tag + "o1_war"))
                    a = nc.scalar.activation(
                        o1[:, j * FTC + tb * 512: j * FTC + tb * 512 + 512],
                        bk.t[:], AF.Gelu, scale=sh1c[:, j:j + 1])
                    atok = p.inc(a, "act")
                    bk.freed = atok
                pe_now = p.now("pe")
                st_h.done(j, pe_now)
            o1_ready = atok
            g_war = pe_now
            st_h2 = WStream(p, ch2, 32, 32, pan, wsl)
            for j in range(32):
                hits = []
                rws = []
                for kt in range(32):
                    b_, c_, r_ = st_h2.get(j, kt)
                    hits.append((b_, c_))
                    rws.append(r_)
                rws = tuple(set(rws))
                for tb in range(NTB):
                    bk = bank()
                    t = p.mm([(bk.t[:], hits[kt][0][:, hits[kt][1]:hits[kt][1] + 128],
                               o1[:, kt * FTC + tb * 512: kt * FTC + tb * 512 + 512],
                               kt == 0, kt == 31) for kt in range(32)],
                             waits=(o1_ready, bk.freed) + rws)
                    bk.written = t
                    p.wait("act", t)
                    p.wait("act", g_war)
                    a = nc.scalar.activation(
                        g[:, j * FTC + tb * 512: j * FTC + tb * 512 + 512],
                        bk.t[:], AF.Gelu, scale=sh2c[:, j:j + 1])
                    atok = p.inc(a, "act")
                    bk.freed = atok
                pe_now = p.now("pe")
                st_h2.done(j, pe_now)
            o2_ready = atok
            p.tok[tag + "o1_war"] = pe_now
            st_w3 = WStream(p, cw3, 32, 8, pan, wsl)
            for f in range(8):
                bk0, bk1_ = bank(), bank()
                items = []
                rws = []
                for kt in range(32):
                    b_, c_, r_ = st_w3.get(f, kt)
                    rws.append(r_)
                    l = b_[:, c_:c_ + 128]
                    items.append((bk0.t[:], l, g[:, kt * FTC + 0:kt * FTC + 512],
                                  kt == 0, kt == 31))
                    items.append((bk1_.t[:], l, g[:, kt * FTC + 512:kt * FTC + 1024],
                                  kt == 0, kt == 31))
                rws = tuple(set(rws))
                t = p.mm(items, waits=(o2_ready, bk0.freed, bk1_.freed) + rws, sgc=True)
                bk0.written = bk1_.written = t
                p.wait("dve", t)
                for tb, bk in ((0, bk0), (1, bk1_)):
                    xsl = x_res[:, f * FTC + tb * 512: f * FTC + tb * 512 + 512]
                    v = nc.vector.scalar_tensor_tensor(xsl, bk.t[:], s3c[:, f:f + 1],
                                                       xsl, ALU.mult, ALU.add)
                    vtok = p.inc(v, "dve")
                    bk.freed = vtok
                pe_now = p.now("pe")
                st_w3.done(f, pe_now)
            p.tok[tag + "g_war"] = pe_now
            x_dst = ap3(dst_t, c0, [[S, 128], [128 * S, 8], [1, FTC]])
            stok = p.dma("sp", x_dst, x_res[:].rearrange("p (k t) -> p k t", k=8), xst,
                         waits=(vtok,))
            xld.freed = stok
    p.barrier()
    return stok



def emit_prep_cache(p, src_t, cache_t, K, O, bufs, ldsl, store_slots):
    KT, OT = K // 128, O // 128
    last = None
    for j in range(OT):
        i = j % 2
        b, ls, ss = bufs[i], ldsl[i], store_slots[i]
        src = ap3(src_t, j * 128, [[O, 128], [128 * O, KT], [1, 128]])
        rdy = p.dma("pool", b[:, 0:KT * 128].rearrange("p (k c) -> p k c", k=KT),
                    src, ls, waits=(ss.ready,))
        dst = ap3(cache_t, j * (128 * KT * 128), [[KT * 128, 128], [1, KT * 128]])
        last = p.dma("pool", dst, b[:, 0:KT * 128], ss, waits=(rdy,))
        ls.freed = last
    return ((store_slots[0].sem, store_slots[0].sem.val),
            (store_slots[1].sem, store_slots[1].sem.val))


def build_program(S=4096, phases=("attn", "ffn1", "mamba", "ffn2"), sim_safe=False):
    nc = bass.Bass("TRN2", target_bir_lowering=False)
    ctx = nc.ctx
    din = {}
    def dram_in(name, shp, dt=F32):
        din[name] = nc.dram_tensor(name, shp, dt, kind="ExternalInput")
        return din[name].ap()

    xT = dram_in("xT", [D, S])
    wqkvT = dram_in("wqkvT", [D, 3 * D])
    woutT = dram_in("woutT", [D, D])
    w1T = dram_in("w1T", [D, DH]); w2T = dram_in("w2T", [D, DH])
    h1T = dram_in("h1T", [DH, DH]); h2T = dram_in("h2T", [DH, DH])
    w3T = dram_in("w3T", [DH, D])
    win_w = dram_in("win_w", [D, 2 * DI])
    wx_w = dram_in("wx_w", [DI, DTRANK + 2 * DSTATE])
    wdt_w = dram_in("wdt_w", [DTRANK, DI])
    woutm = dram_in("woutm", [DI, D])
    bqkv_c = dram_in("bqkv_c", [128, 24]); bout_c = dram_in("bout_c", [128, 8])
    bqkv_raw = dram_in("bqkv_raw", [1, 3 * D])
    bdt_c = dram_in("bdt_c", [128, 16]); dpar_c = dram_in("dpar_c", [128, 16])
    convb_c = dram_in("convb_c", [128, 16]); convw_c = dram_in("convw_c", [128, 64])
    alog_c = dram_in("alog_c", [128, 256])
    s1_c = dram_in("s1_c", [128, 32]); s2_c = dram_in("s2_c", [128, 32])
    sh1_c = dram_in("sh1_c", [128, 32]); sh2_c = dram_in("sh2_c", [128, 32])
    s3_c = dram_in("s3_c", [128, 8])
    ident_in = dram_in("ident_in", [128, 128], BF16)
    alibi_in = dram_in("alibi_in", [128, WIN])
    onescol_in = dram_in("onescol_in", [128, 1])
    onesrow_in = dram_in("onesrow_in", [1, 128])

    out_t = nc.dram_tensor("xT_out", [D, S], F32, kind="ExternalOutput").ap()
    x1 = nc.dram_tensor("x1", [D, S], F32, kind="Internal").ap()
    x2 = nc.dram_tensor("x2", [D, S], F32, kind="Internal").ap()
    x3 = nc.dram_tensor("x3", [D, S], F32, kind="Internal").ap()
    cqkv = nc.dram_tensor("cqkv", [24 * 128 * 8 * 128], BF16, kind="Internal").ap()
    cwout = nc.dram_tensor("cwout", [8 * 128 * 8 * 128], BF16, kind="Internal").ap()
    cw1 = nc.dram_tensor("cw1", [32 * 128 * 8 * 128], BF16, kind="Internal").ap()
    cw2 = nc.dram_tensor("cw2", [32 * 128 * 8 * 128], BF16, kind="Internal").ap()
    ch1 = nc.dram_tensor("ch1", [32 * 128 * 32 * 128], BF16, kind="Internal").ap()
    ch2 = nc.dram_tensor("ch2", [32 * 128 * 32 * 128], BF16, kind="Internal").ap()
    cw3 = nc.dram_tensor("cw3", [8 * 128 * 32 * 128], BF16, kind="Internal").ap()
    cwin = nc.dram_tensor("cwin", [32 * 128 * 8 * 128], BF16, kind="Internal").ap()
    cwoutm = nc.dram_tensor("cwoutm", [8 * 128 * 16 * 128], BF16, kind="Internal").ap()
    zst = nc.dram_tensor("zst", [16 * 128 * S], BF16, kind="Internal").ap()
    bst = nc.dram_tensor("bst", [32768], F32, kind="Internal").ap()

    bctx = ExitStack()
    p = P(nc, bctx, sim_safe=sim_safe)
    sbp = lambda n, shp, dt=F32: bctx.enter_context(nc.sbuf_tensor(n, shp, dt))
    consts = {}
    consts["onescol"] = sbp("onescol", [128, 1])
    consts["onesrow"] = sbp("onesrow", [1, 128])
    consts["ident"] = sbp("ident", [128, 128], BF16)
    bqkv = sbp("bqkv", [128, 24]); bout = sbp("bout", [128, 8])
    bdt = sbp("bdt", [128, 16]); dpar = sbp("dpar", [128, 16])
    convb = sbp("convb", [128, 16]); convw = sbp("convw", [128, 64])
    aneg = sbp("aneg", [128, 256])
    s1s = sbp("s1s", [128, 32]); s2s = sbp("s2s", [128, 32])
    sh1s = sbp("sh1s", [128, 32]); sh2s = sbp("sh2s", [128, 32])
    s3s = sbp("s3s", [128, 8])

    psl = p.slot("prep")
    for dst, src in ((consts["onescol"], onescol_in), (consts["onesrow"], onesrow_in),
                     (consts["ident"], ident_in),
                     (bqkv, bqkv_c), (bout, bout_c), (bdt, bdt_c), (dpar, dpar_c),
                     (convb, convb_c), (convw, convw_c), (aneg, alog_c),
                     (s1s, s1_c), (s2s, s2_c), (sh1s, sh1_c), (sh2s, sh2_c),
                     (s3s, s3_c)):
        p.dma("sp", dst[:], src, psl)
    p.wait("act", psl.ready)
    a = nc.scalar.activation(aneg[:], aneg[:], AF.Exp)
    at = p.inc(a, "act")
    p.wait("dve", at)
    v = nc.vector.tensor_scalar(aneg[:], aneg[:], -1.0, None, ALU.mult)
    p.tok["const_rdy"] = (psl.ready, p.inc(v, "dve"))

    pctx = ExitStack()
    pbuf = [pctx.enter_context(nc.sbuf_tensor(f"prepb{i}", [128, 32 * 128], BF16))
            for i in range(2)]
    ldsl = [p.slot(f"prepld{i}") for i in range(2)]
    stsl = [p.slot(f"prepst{i}") for i in range(2)]
    cache_tok = {}
    specs = [("cqkv", wqkvT, cqkv, D, 3 * D), ("cwout", woutT, cwout, D, D),
             ("cw1", w1T, cw1, D, DH), ("cw2", w2T, cw2, D, DH),
             ("ch1", h1T, ch1, DH, DH), ("ch2", h2T, ch2, DH, DH),
             ("cw3", w3T, cw3, DH, D), ("cwin", win_w, cwin, D, 2 * DI),
             ("cwoutm", woutm, cwoutm, DI, D)]
    for name, srct, cache, K, O in specs:
        cache_tok[name] = emit_prep_cache(p, srct.tensor, cache.tensor, K, O,
                                          pbuf, ldsl, stsl)
    p.tok["cache_tok"] = cache_tok

    scratch = [x1, x2, x3]
    si = 0
    last_store = None
    cur = xT
    if "attn" not in phases:
        p.barrier()
        pctx.close()
    for ph in phases:
        dst = out_t if ph == phases[-1] else scratch[si]
        si += 1
        if ph in ("ffn1", "ffn2"):
            last_store = emit_ffn(p, S, cur.tensor, dst.tensor,
                                  (cw1.tensor, cw2.tensor, ch1.tensor, ch2.tensor,
                                   cw3.tensor),
                                  (s1s, s2s, sh1s, sh2s, s3s), consts, ph + "_")
        elif ph == "attn":
            last_store = emit_attn(p, S, cur.tensor, dst.tensor, cqkv.tensor,
                                   cwout.tensor, bqkv, bout, bqkv_raw.tensor,
                                   alibi_in.tensor, consts, "at_")
            pctx.close()
        elif ph == "mamba":
            last_store = emit_mamba(p, S, cur.tensor, dst.tensor, cwin.tensor,
                                    cwoutm.tensor, wx_w.tensor, wdt_w.tensor,
                                    zst.tensor, bst.tensor, bdt, dpar, convb, convw,
                                    aneg, consts, "mb_")
        cur = dst
    p.wait("sp", last_store)
    p.barrier()
    bctx.close()
    return nc



def _cols(v, n):
    return np.ascontiguousarray(v.reshape(n, 128).T.astype(np.float32))


def make_inputs(ins, S=4096):
    asc = np.ascontiguousarray
    f = lambda a: asc(np.asarray(a, dtype=np.float32))
    shared = {
        "wqkvT": asc(f(ins["L0_attn_a_w_qkv"]).T),
        "woutT": asc(f(ins["L0_attn_a_w_out"]).T),
        "w1T": asc(f(ins["L0_ffn_w1_w"]).T), "w2T": asc(f(ins["L0_ffn_w2_w"]).T),
        "h1T": asc(f(ins["L0_ffn_h1_w"]).T), "h2T": asc(f(ins["L0_ffn_h2_w"]).T),
        "w3T": asc(f(ins["L0_ffn_w3_w"]).T),
        "win_w": f(ins["L0_attn_b_W_in"]), "wx_w": f(ins["L0_attn_b_W_x"]),
        "wdt_w": f(ins["L0_attn_b_W_dt"]), "woutm": f(ins["L0_attn_b_W_out"]),
        "bqkv_c": _cols(f(ins["L0_attn_a_b_qkv"]), 24),
        "bout_c": _cols(f(ins["L0_attn_a_b_out"]), 8),
        "bqkv_raw": f(ins["L0_attn_a_b_qkv"]).reshape(1, 3 * D),
        "bdt_c": _cols(f(ins["L0_attn_b_b_dt"]), 16),
        "dpar_c": _cols(f(ins["L0_attn_b_D_param"]), 16),
        "convb_c": _cols(f(ins["L0_attn_b_conv_b"]), 16),
        "convw_c": asc(f(ins["L0_attn_b_conv_w"]).reshape(16, 128, 4)
                       .transpose(1, 0, 2).reshape(128, 64)),
        "alog_c": asc(f(ins["L0_attn_b_A_log"]).reshape(16, 128, 16)
                      .transpose(1, 0, 2).reshape(128, 256)),
        "s1_c": _cols(f(ins["L0_ffn_w1_s"]), 32), "s2_c": _cols(f(ins["L0_ffn_w2_s"]), 32),
        "sh1_c": _cols(f(ins["L0_ffn_h1_s"]), 32), "sh2_c": _cols(f(ins["L0_ffn_h2_s"]), 32),
        "s3_c": _cols(f(ins["L0_ffn_w3_s"]), 8),
        "ident_in": np.eye(128, dtype=ml_dtypes.bfloat16),
        "onescol_in": np.ones((128, 1), np.float32),
        "onesrow_in": np.ones((1, 128), np.float32),
    }
    i = np.arange(128)[:, None]
    j = np.arange(WIN)[None, :]
    rel = j - (WIN - 128) - i
    alibi = np.where(rel <= 0, rel.astype(np.float32), np.float32(-1e30))
    shared["alibi_in"] = np.ascontiguousarray(alibi.astype(np.float32))
    return shared


def kernel(**ins):
    S = ins["L0_attn_a_x"].shape[1]
    B = ins["L0_attn_a_x"].shape[0]
    shared = make_inputs(ins, S)
    x = np.asarray(ins["L0_attn_a_x"], dtype=np.float32)
    in_maps = []
    for b in range(B):
        m = dict(shared)
        m["xT"] = np.ascontiguousarray(x[b].T)
        in_maps.append(m)
    nc = build_program(S=S)
    res = run_bass_kernel_spmd(nc, in_maps, core_ids=list(range(B)))
    out = np.stack([np.ascontiguousarray(r["xT_out"].T) for r in res.results], axis=0)
    return out



def emit_attn(p, S, src_t, dst_t, cqkv, cwout, bqkv, bout, bqkv_raw_t, alibi_t,
              consts, tag):
    nc = p.nc
    ATC = 1024
    NTB = 2
    nchunks = S // ATC
    KW = 1536
    with ExitStack() as ctx:
        sb = lambda n, shp, dt: ctx.enter_context(nc.sbuf_tensor(tag + n, shp, dt))
        ps = lambda n, shp, dt=F32: ctx.enter_context(nc.psum_tensor(tag + n, shp, dt))
        QT = sb("QT", [128, 8 * ATC], BF16)
        KT = sb("KT", [128, 8 * KW], BF16)
        V = sb("V", [128, 12 * 1024], BF16)
        OT = sb("OT", [128, 8 * ATC], BF16)
        xn = sb("xn", [128, 8 * ATC], BF16)
        x_res = sb("x", [128, 8 * ATC], F32)
        pan = [sb(f"pan{i}", [128, 2048], BF16) for i in range(4)]
        bvb = sb("bvb", [128, D], F32)
        alibi = sb("alibi", [128, WIN], F32)
        sc_sb = sb("scs", [128, 4 * WIN], F32)
        pbf = [sb(f"pbf{i}", [128, WIN], BF16) for i in range(4)]
        pt_sb = [sb(f"pt{i}", [128, WIN], BF16) for i in range(4)]
        mlc = sb("mlc", [128, 16], F32)
        sq = [sb(f"sq{i}", [128, 512], F32) for i in range(2)]
        banks = [Bank(ps(f"pb{i}", [128, 512])) for i in range(2)]
        sh = make_ln_shared(p, ctx, tag + "L", consts, sq, bc=(banks[0], banks[1]))
        scp1 = Bank(ps("scp", [128, WIN]))
        ptp2 = ps("ptp", [128, 2048], BF16)
        ptp_ = [Bank(ptp2) for _ in range(2)]
        otpT = ps("otp", [64, 1024])
        otp_ = [Bank(otpT) for _ in range(2)]
        xld = p.slot(tag + "xld")
        xst = p.slot(tag + "xst")
        wsl = [p.slot(tag + f"w{i}") for i in range(4)]
        csl = p.slot(tag + "cst")
        bank_i = [0]

        def bank():
            b = banks[bank_i[0] % 2]
            bank_i[0] += 1
            return b

        p.dma("sp", alibi[:], ap3(alibi_t, 0, [[WIN, 128], [1, WIN]]), csl)
        p.dma("sp", bvb[:], ap3(bqkv_raw_t, 2 * D, [[0, 128], [1, D]]), csl)
        const_tok = csl.ready

        vtok = stok = None
        for c in range(nchunks):
            c0 = c * ATC
            cb_tok = None
            if c > 0:
                deps = (p.tok.get(tag + "kv_pe_war"), p.tok.get(tag + "kv_ready"),
                        p.tok.get(tag + "v_ready"))
                t = None
                for hp in range(8):
                    t = p.ve(deps, nc.vector.tensor_copy,
                             KT[:, hp * KW: hp * KW + 512],
                             KT[:, hp * KW + 1024: hp * KW + 1536])
                    deps = ()
                for blk in range(4):
                    t = p.ve((), nc.vector.tensor_copy,
                             V[:, blk * 1024: blk * 1024 + 1024],
                             V[:, (8 + blk) * 1024: (8 + blk) * 1024 + 1024])
                cb_tok = t
            p.dma("sp", x_res[:].rearrange("p (k t) -> p k t", k=8),
                  ap3(src_t, c0, [[S, 128], [128 * S, 8], [1, ATC]]), xld)
            ln = None
            for tb in range(NTB):
                ln = emit_ln(p, sh, x_res, xld.ready, xn, tb * 512, tb * 512,
                             ATC, ATC, xn_war=p.tok.get(tag + "xn_war"))
            st = WStream(p, cqkv, 8, 24, pan, wsl, rdy=p.tok["cache_tok"]["cqkv"])
            qk_ev = None
            for j in range(16):
                qits = []
                rws = []
                for kt in range(8):
                    b_, c_, r_ = st.get(j, kt)
                    qits.append((b_, c_))
                    rws.append(r_)
                rws = tuple(set(rws))
                for tb in range(NTB):
                    bk = bank()
                    t = p.mm([(bk.t[:], qits[kt][0][:, qits[kt][1]:qits[kt][1] + 128],
                               xn[:, kt * ATC + tb * 512: kt * ATC + tb * 512 + 512],
                               kt == 0, kt == 7) for kt in range(8)],
                             waits=(ln, bk.freed) + rws)
                    bk.written = t
                    if j < 8:
                        dstap = QT[:, j * ATC + tb * 512: j * ATC + tb * 512 + 512]
                    else:
                        dstap = KT[:, (j - 8) * KW + 512 + tb * 512:
                                   (j - 8) * KW + 512 + tb * 512 + 512]
                    qk_ev = p.ae((t, cb_tok, p.tok.get(tag + "kv_pe_war"),
                                  p.tok.get("const_rdy")),
                                 nc.scalar.activation, dstap, bk.t[:], AF.Identity,
                                 bias=bqkv[:, j:j + 1])
                    bk.freed = qk_ev
                pe_now = p.now("pe")
                st.done(j, pe_now)
            v_ev = None
            for grp in range(2):
                vits = {}
                rws = []
                for ot in range(4):
                    for kt in range(8):
                        b_, c_, r_ = st.get(16 + grp * 4 + ot, kt)
                        vits[(ot, kt)] = (b_, c_)
                        rws.append(r_)
                rws = tuple(set(rws))
                for tt in range(8):
                    bk = bank()
                    items = []
                    for ot in range(4):
                        for kt in range(8):
                            b_, c_ = vits[(ot, kt)]
                            items.append((bk.t[:, ot * 128: ot * 128 + 128],
                                          xn[:, kt * ATC + tt * 128: kt * ATC + tt * 128 + 128],
                                          b_[:, c_:c_ + 128],
                                          kt == 0, kt == 7))
                    t = p.mm(items, waits=(ln, bk.freed) + rws, sgc=True)
                    bk.written = t
                    v_ev = p.ve((t, cb_tok, const_tok, p.tok.get(tag + "kv_pe_war")),
                                nc.vector.scalar_tensor_tensor,
                                V[:, (4 + tt) * 1024 + grp * 512:
                                  (4 + tt) * 1024 + grp * 512 + 512],
                                bk.t[:], 1.0, bvb[:, grp * 512: grp * 512 + 512],
                                ALU.mult, ALU.add)
                    bk.freed = v_ev
                pe_now = p.now("pe")
                for ot in range(4):
                    st.done(16 + grp * 4 + ot, pe_now)
            p.tok[tag + "xn_war"] = pe_now
            p.tok[tag + "kv_ready"] = qk_ev
            p.tok[tag + "v_ready"] = v_ev
            iters = [(h, qt) for h in range(NHEADS) for qt in range(8)]
            stage1_out = {}
            prev_tp = [None] * 4
            prev_trl = [None] * 4
            prev_tn = [None] * 4

            geom = {}

            def stage0(idx):
                h, qt = iters[idx]
                hp, hh = h // 2, (h % 2) * 64
                q_abs = c * 8 + qt
                kend = q_abs + 1
                kstart = max(0, kend - 4)
                nk = kend - kstart
                w0 = (kstart - (c * 8 - 4)) * 128
                geom[idx] = (nk, kstart)
                lhsT = QT[hh:hh + 64, hp * ATC + qt * 128: hp * ATC + qt * 128 + 128]
                t_sc = p.mm([(scp1.t[:, 0:nk * 128], lhsT,
                              KT[hh:hh + 64, hp * KW + w0: hp * KW + w0 + nk * 128],
                              True, True)],
                            waits=(qk_ev, scp1.freed))
                scp1.written = t_sc

            def stage1(idx):
                sl = idx % 4
                nk, _ = geom[idx]
                t_ev = p.ve((scp1.written,) + p.tok.get(tag + "sc_war" + str(sl), ()),
                            nc.vector.scalar_tensor_tensor,
                            sc_sb[:, sl * WIN: sl * WIN + nk * 128],
                            scp1.t[:, 0:nk * 128], 0.125,
                            alibi[:, WIN - nk * 128:WIN], ALU.mult, ALU.add)
                scp1.freed = t_ev
                t_m = p.ve((t_ev, prev_tp[sl]), nc.vector.reduce_max,
                           mlc[:, sl * 4: sl * 4 + 1],
                           sc_sb[:, sl * WIN: sl * WIN + nk * 128], AX.X, negate=True)
                stage1_out[idx] = [t_m, None, None, None]

            def stage2(idx):
                sl = idx % 4
                nk, _ = geom[idx]
                t_m = stage1_out[idx][0]
                t_p = p.ae((t_m, prev_trl[sl], p.tok.get((tag + "pbw", sl))),
                           nc.scalar.activation, pbf[sl][:, 0:nk * 128],
                           sc_sb[:, sl * WIN: sl * WIN + nk * 128], AF.Exp,
                           bias=mlc[:, sl * 4: sl * 4 + 1],
                           accum_out=mlc[:, sl * 4 + 1: sl * 4 + 2])
                p.tok[tag + "sc_war" + str(sl)] = (t_m, t_p)
                prev_tp[sl] = t_p
                stage1_out[idx][1] = t_p

            def stage3(idx):
                sl = idx % 4
                t_p = stage1_out[idx][1]
                t_rl = p.ve((t_p, prev_tn[sl]), nc.vector.reciprocal,
                            mlc[:, sl * 4 + 2: sl * 4 + 3],
                            mlc[:, sl * 4 + 1: sl * 4 + 2])
                prev_trl[sl] = t_rl
                stage1_out[idx][2] = t_rl

            def stage4(idx):
                sl = idx % 4
                nk, _ = geom[idx]
                t_rl = stage1_out[idx][2]
                t_n = p.ae((t_rl,), nc.scalar.activation, pbf[sl][:, 0:nk * 128],
                           pbf[sl][:, 0:nk * 128], AF.Identity,
                           scale=mlc[:, sl * 4 + 2: sl * 4 + 3])
                prev_tn[sl] = t_n
                stage1_out[idx][3] = t_n

            def stage5(idx):
                nonlocal ot_ev
                h, qt = iters[idx]
                sl = idx % 4
                psl_ = idx % 2
                ptp = ptp_[psl_]
                pto = psl_ * 1024
                otp = otp_[psl_]
                oto = psl_ * 512
                hp, hh = h // 2, (h % 2) * 64
                nk, kstart = geom.pop(idx)
                t_n = stage1_out.pop(idx)[3]
                p.wait("pe", t_n)
                p.wait("pe", ptp.freed)
                p.wait("pe", p.tok.get("const_rdy"))
                inst = None
                for b in range(nk):
                    inst = nc.tensor.transpose(
                        ptp.t[:, pto + b * 128: pto + b * 128 + 128],
                        pbf[sl][:, b * 128: b * 128 + 128], consts["ident"][:])
                t_tr = p.inc(inst, "pe")
                ptp.written = t_tr
                p.tok[(tag + "pbw", sl)] = t_tr
                t_pt = p.ve((t_tr, p.tok.get((tag + "ptw", psl_))),
                            nc.vector.tensor_copy, pt_sb[sl][:, 0:nk * 128],
                            ptp.t[:, pto:pto + nk * 128])
                ptp.freed = t_pt
                items = []
                for b in range(nk):
                    vblk = kstart - (c * 8 - 4) + b
                    items.append((otp.t[:, oto:oto + 128],
                                  V[:, vblk * 1024 + h * 64: vblk * 1024 + h * 64 + 64],
                                  pt_sb[sl][:, b * 128: b * 128 + 128],
                                  b == 0, b == nk - 1))
                t_pv = p.mm(items, waits=(t_pt, otp.freed, v_ev))
                otp.written = t_pv
                p.tok[(tag + "ptw", psl_)] = t_pv
                ot_ev = p.ve((t_pv, p.tok.get(tag + "ot_war")),
                             nc.vector.tensor_copy,
                             OT[hh:hh + 64, hp * ATC + qt * 128: hp * ATC + qt * 128 + 128],
                             otp.t[0:64, oto:oto + 128])
                otp.freed = ot_ev

            ot_ev = None
            N_IT = len(iters)
            stages = (stage0, stage1, stage2, stage3, stage4, stage5)
            for idx in range(N_IT + 5):
                for d in range(5, -1, -1):
                    k = idx - d
                    if 0 <= k < N_IT:
                        stages[d](k)
            pe_now = p.now("pe")
            p.tok[tag + "kv_pe_war"] = pe_now
            p.tok[tag + "cb_war"] = pe_now
            st_o = WStream(p, cwout, 8, 8, pan, wsl, rdy=p.tok["cache_tok"]["cwout"])
            for f in range(8):
                oits = []
                rws = []
                for kt in range(8):
                    b_, c_, r_ = st_o.get(f, kt)
                    oits.append((b_, c_))
                    rws.append(r_)
                rws = tuple(set(rws))
                for tb in range(NTB):
                    bk = bank()
                    t = p.mm([(bk.t[:], oits[kt][0][:, oits[kt][1]:oits[kt][1] + 128],
                               OT[:, kt * ATC + tb * 512: kt * ATC + tb * 512 + 512],
                               kt == 0, kt == 7) for kt in range(8)],
                             waits=(ot_ev, bk.freed) + rws)
                    bk.written = t
                    xsl = x_res[:, f * ATC + tb * 512: f * ATC + tb * 512 + 512]
                    vtok = p.ve((t, p.tok.get("const_rdy")),
                                nc.vector.scalar_tensor_tensor, xsl, bk.t[:],
                                bout[:, f:f + 1], xsl, ALU.add, ALU.add)
                    bk.freed = vtok
                pe_now = p.now("pe")
                st_o.done(f, pe_now)
            p.tok[tag + "ot_war"] = pe_now
            stok = p.dma("sp", ap3(dst_t, c0, [[S, 128], [128 * S, 8], [1, ATC]]),
                         x_res[:].rearrange("p (k t) -> p k t", k=8), xst,
                         waits=(vtok,))
            xld.freed = stok
    p.barrier()
    return stok



def emit_mamba(p, S, src_t, dst_t, cwin, cwoutm, wx_t, wdt_t, zst_t, bst_t,
               bdt, dpar, convb, convw, aneg, consts, tag):
    nc = p.nc
    MTC = 512
    nchunks = S // MTC
    with ExitStack() as ctx:
        sb = lambda n, shp, dt: ctx.enter_context(nc.sbuf_tensor(tag + n, shp, dt))
        ps = lambda n, shp, dt=F32: ctx.enter_context(nc.psum_tensor(tag + n, shp, dt))
        xn = sb("xn", [128, 8 * 512], BF16)
        xb = sb("xb", [128, 16 * 515], BF16)
        halo = sb("halo", [128, 48], BF16)
        xc = sb("xc", [128, 16 * 512], F32)
        dt_h = sb("dt", [128, 8 * 512], F32)
        u_h = sb("u", [128, 8 * 512], F32)
        ya_h = sb("ya", [128, 8 * 512], F32)
        ybf = sb("ybf", [128, 8 * 512], BF16)
        x_res = sb("x", [128, 8 * 512], F32)
        state = sb("st", [128, 256], F32)
        wx_sb = sb("wx", [128, 16 * 96], BF16)
        wdt_sb = sb("wdt", [64, 2048], BF16)
        pan = [sb(f"pan{i}", [128, 2048], BF16) for i in range(4)]
        dtl = sb("dtl", [64, 512], BF16)
        brow = sb("brow", [16, 512], F32)
        crow = sb("crow", [16, 512], F32)
        bbc = [sb(f"bbc{i}", [128, 512], F32) for i in range(2)]
        cbc = [sb(f"cbc{i}", [128, 512], F32) for i in range(2)]
        at2 = [sb(f"at{i}", [128, 512], F32) for i in range(3)]
        bt1 = sb("bt", [128, 512], F32)
        ht2 = [sb(f"ht{i}", [128, 512], F32) for i in range(4)]
        ctmp = sb("ctmp", [128, 512], F32)
        stmp2 = [sb(f"stmp{i}", [128, 512], F32) for i in range(2)]
        z4 = sb("z4", [128, 4 * 512], BF16)
        zrd = [sb(f"zrd{i}", [128, 512], BF16) for i in range(2)]
        sq = [sb(f"sq{i}", [128, 512], F32) for i in range(2)]
        banks = [Bank(ps(f"pb{i}", [128, 512])) for i in range(2)]
        sh = make_ln_shared(p, ctx, tag + "L", consts, sq, bc=(banks[0], banks[1]))
        pdt = Bank(ps("pdt", [64, 512]))
        pB = Bank(ps("pB", [16, 512]))
        pC = Bank(ps("pC", [16, 512]))
        xld = p.slot(tag + "xld")
        xst = p.slot(tag + "xst")
        wsl = [p.slot(tag + f"w{i}") for i in range(4)]
        csl = p.slot(tag + "cst")
        zw = p.slot(tag + "zw")
        zr = [p.slot(tag + f"zr{i}") for i in range(2)]
        bw = p.slot(tag + "bw")
        bcs = [p.slot(tag + f"bc{i}") for i in range(2)]
        bank_i = [0]

        def bank():
            b = banks[bank_i[0] % 2]
            bank_i[0] += 1
            return b

        p.dma("pool", wx_sb[:].rearrange("p (k c) -> p k c", k=16),
              ap3(wx_t, 0, [[96, 128], [128 * 96, 16], [1, 96]]), csl)
        p.dma("pool", wdt_sb[:], ap3(wdt_t, 0, [[DI, 64], [1, DI]]), csl)
        wconst = csl.ready
        hz = p.ve((), nc.vector.memset, halo[:], 0.0)
        sz = p.ve((), nc.vector.memset, state[:], 0.0)
        p.tok[tag + "halo"] = hz
        for n in range(16):
            for gi in range(16):
                p.tok[(tag + "st", n, gi)] = sz

        vtok = stok = None
        for c in range(nchunks):
            c0 = c * MTC
            p.dma("sp", x_res[:].rearrange("p (k t) -> p k t", k=8),
                  ap3(src_t, c0, [[S, 128], [128 * S, 8], [1, MTC]]), xld)
            ln = emit_ln(p, sh, x_res, xld.ready, xn, 0, 0, 512, 512,
                         xn_war=p.tok.get(tag + "xn_war"))
            st_in = WStream(p, cwin, 8, 32, pan, wsl,
                            rdy=p.tok["cache_tok"]["cwin"])
            hin = p.ve((p.tok[tag + "halo"], p.tok.get(tag + "xb_war")),
                       nc.vector.tensor_copy,
                       xb[:].rearrange("p (i c) -> p i c", i=16)[:, :, 0:3],
                       halo[:].rearrange("p (i c) -> p i c", i=16))
            xb_ev = None
            z_ev = None
            for j in range(32):
                xits = []
                rws = []
                for kt in range(8):
                    b_, c_, r_ = st_in.get(j, kt)
                    xits.append((b_, c_))
                    rws.append(r_)
                rws = tuple(set(rws))
                bk = bank()
                t = p.mm([(bk.t[:], xits[kt][0][:, xits[kt][1]:xits[kt][1] + 128],
                           xn[:, kt * 512: kt * 512 + 512],
                           kt == 0, kt == 7) for kt in range(8)],
                         waits=(ln, bk.freed) + rws)
                bk.written = t
                if j < 16:
                    xb_ev = p.ae((t, hin, p.tok.get(tag + "xb_war")),
                                 nc.scalar.copy,
                                 xb[:, j * 515 + 3: j * 515 + 515], bk.t[:])
                    bk.freed = xb_ev
                else:
                    jz = j - 16
                    z_ev = p.ae((t, p.tok.get((tag + "z4w", jz % 4))),
                                nc.scalar.copy,
                                z4[:, (jz % 4) * 512: (jz % 4) * 512 + 512], bk.t[:])
                    bk.freed = z_ev
                    if jz % 4 == 3:
                        zt = p.dma("sp",
                                   ap3(zst_t, (jz - 3) * 128 * S + c0,
                                       [[S, 128], [128 * S, 4], [1, 512]]),
                                   z4[:].rearrange("p (i c) -> p i c", i=4),
                                   zw, waits=(z_ev,))
                        for q in range(4):
                            p.tok[(tag + "z4w", q)] = zt
                        p.tok[(tag + "zdram", jz // 4)] = zt
                pe_now = p.now("pe")
                st_in.done(j, pe_now)
            p.tok[tag + "xn_war"] = pe_now
            hout = p.ve((xb_ev,), nc.vector.tensor_copy,
                        halo[:].rearrange("p (i c) -> p i c", i=16),
                        xb[:].rearrange("p (i c) -> p i c", i=16)[:, :, 512:515])
            p.tok[tag + "halo"] = hout
            conv_toks = []
            for i in range(16):
                xci = xc[:, i * 512: i * 512 + 512]
                tconv = p.ve((xb_ev, hin, p.tok.get((tag + "xc_war", i))),
                             nc.vector.tensor_scalar, xci,
                             xb[:, i * 515 + 0: i * 515 + 512],
                             convw[:, i * 4: i * 4 + 1], convb[:, i: i + 1],
                             ALU.mult, ALU.add)
                for k in range(1, 4):
                    tconv = p.ve((tconv,), nc.vector.scalar_tensor_tensor, xci,
                                 xb[:, i * 515 + k: i * 515 + k + 512],
                                 convw[:, i * 4 + k: i * 4 + k + 1], xci,
                                 ALU.mult, ALU.add)
                tsil = p.ae((tconv,), nc.scalar.activation, xci, xci, AF.Silu)
                conv_toks.append(tsil)
            p.wait("pe", pdt.freed)
            p.wait("pe", pB.freed)
            p.wait("pe", pC.freed)
            xdbl_items_done = None
            for kt in range(16):
                sl = kt % 2
                tcast = p.ae((conv_toks[kt], p.tok.get((tag + "xcb_war", sl))),
                             nc.scalar.copy, zrd[sl][:],
                             xc[:, kt * 512: kt * 512 + 512])
                t = p.mm([(pdt.t[:], wx_sb[:, kt * 96 + 0: kt * 96 + 64],
                           zrd[sl][:], kt == 0, kt == 15),
                          (pB.t[:], wx_sb[:, kt * 96 + 64: kt * 96 + 80],
                           zrd[sl][:], kt == 0, kt == 15),
                          (pC.t[:], wx_sb[:, kt * 96 + 80: kt * 96 + 96],
                           zrd[sl][:], kt == 0, kt == 15)],
                         waits=(tcast, wconst), sgc=True)
                p.tok[(tag + "xcb_war", sl)] = t
                xdbl_items_done = t
            pdt.written = pB.written = pC.written = xdbl_items_done
            t_dtl = p.ae((xdbl_items_done, p.tok.get(tag + "dtl_war")),
                         nc.scalar.copy, dtl[:], pdt.t[:])
            pdt.freed = t_dtl
            t_br = p.ve((xdbl_items_done, p.tok.get(tag + "brow_war")),
                        nc.vector.tensor_copy, brow[:], pB.t[:])
            pB.freed = t_br
            t_cr = p.ve((t_br,), nc.vector.tensor_copy, crow[:], pC.t[:])
            pC.freed = t_cr
            par = c % 2
            bwt = p.dma("sp", ap3(bst_t, par * 16384, [[512, 16], [1, 512]]),
                        brow[:], bw, waits=(t_br,))
            bwt = p.dma("sp", ap3(bst_t, par * 16384 + 8192, [[512, 16], [1, 512]]),
                        crow[:], bw, waits=(t_cr,))
            p.tok[tag + "brow_war"] = bwt
            for half in range(2):
                st_wo = WStream(p, cwoutm, 16, 8, pan, wsl,
                                rdy=p.tok["cache_tok"]["cwoutm"])
                dt_toks = []
                for ot in range(8):
                    go = half * 8 + ot
                    bk = bank()
                    t = p.mm([(bk.t[:], wdt_sb[:, go * 128: go * 128 + 128],
                               dtl[:], True, True)],
                             waits=(t_dtl, wconst, bk.freed))
                    bk.written = t
                    ssl = ot % 2
                    tsg = p.ae((t, p.tok.get((tag + "stmp_war", ssl))),
                               nc.scalar.activation, stmp2[ssl][:], bk.t[:],
                               AF.Sigmoid, bias=bdt[:, go: go + 1])
                    bk.freed = tsg
                    taf = p.ve((tsg, p.tok.get((tag + "dt_war", ot))),
                               nc.vector.tensor_scalar,
                               dt_h[:, ot * 512: ot * 512 + 512], stmp2[ssl][:],
                               0.099, 0.001, ALU.mult, ALU.add)
                    p.tok[(tag + "stmp_war", ssl)] = taf
                    dt_toks.append(taf)
                u_toks, ya_toks = [], []
                for ot in range(8):
                    gi = half * 8 + ot
                    tu = p.ve((dt_toks[ot], conv_toks[gi],
                               p.tok.get((tag + "u_war", ot))),
                              nc.vector.tensor_mul,
                              u_h[:, ot * 512: ot * 512 + 512],
                              dt_h[:, ot * 512: ot * 512 + 512],
                              xc[:, gi * 512: gi * 512 + 512])
                    ty = p.ve((conv_toks[gi], p.tok.get((tag + "ya_war", ot))),
                              nc.vector.tensor_scalar,
                              ya_h[:, ot * 512: ot * 512 + 512],
                              xc[:, gi * 512: gi * 512 + 512],
                              dpar[:, gi: gi + 1], None, ALU.mult)
                    u_toks.append(tu)
                    ya_toks.append(ty)
                ya_tok = list(ya_toks)
                bct_tok = [None, None]

                def issue_bcast(n):
                    if n >= 16:
                        return
                    sl = n % 2
                    p.dma("sp", bbc[sl][:],
                          ap3(bst_t, par * 16384 + n * 512, [[0, 128], [1, 512]]),
                          bcs[sl], waits=(bwt,))
                    bct_tok[sl] = p.dma("sp", cbc[sl][:],
                                        ap3(bst_t, par * 16384 + 8192 + n * 512,
                                            [[0, 128], [1, 512]]),
                                        bcs[sl], waits=())

                issue_bcast(0)
                issue_bcast(1)
                pend_st = []

                def flush_state(limit):
                    while len(pend_st) > limit:
                        nn, ggi, hh2, tsc = pend_st.pop(0)
                        tst = p.ae((tsc, p.tok.get((tag + "stw",))),
                                   nc.scalar.copy,
                                   state[:, nn * 16 + ggi: nn * 16 + ggi + 1],
                                   ht2[hh2][:, 511:512])
                        p.tok[(tag + "stw",)] = tst
                        p.tok[(tag + "hst", hh2)] = tst
                        p.tok[(tag + "st", nn, ggi)] = tst

                for n in range(16):
                    sl = n % 2
                    bct = bct_tok[sl]
                    for ot in range(8):
                        gi = half * 8 + ot
                        asl = (n * 8 + ot) % 3
                        ta = p.ae((dt_toks[ot], p.tok.get((tag + "a_war", asl))),
                                  nc.scalar.activation, at2[asl][:],
                                  dt_h[:, ot * 512: ot * 512 + 512], AF.Exp,
                                  scale=aneg[:, gi * 16 + n: gi * 16 + n + 1])
                        tb = p.ve((u_toks[ot], bct, p.tok.get((tag + "b_war",))),
                                  nc.vector.tensor_mul, bt1[:],
                                  u_h[:, ot * 512: ot * 512 + 512], bbc[sl][:])
                        hsl = (n * 8 + ot) % 4
                        tscan = p.ve((ta, tb, p.tok.get((tag + "h_war", hsl)),
                                      p.tok.get((tag + "hst", hsl)),
                                      p.tok[(tag + "st", n, gi)]),
                                     nc.vector.tensor_tensor_scan, ht2[hsl][:],
                                     at2[asl][:], bt1[:],
                                     state[:, n * 16 + gi: n * 16 + gi + 1],
                                     ALU.mult, ALU.add)
                        p.tok[(tag + "a_war", asl)] = tscan
                        p.tok[(tag + "b_war",)] = tscan
                        pend_st.append((n, gi, hsl, tscan))
                        flush_state(2)
                        tc = p.ve((tscan, p.tok.get(tag + "ctmp_war")),
                                  nc.vector.tensor_mul, ctmp[:],
                                  ht2[hsl][:], cbc[sl][:])
                        p.tok[(tag + "h_war", hsl)] = tc
                        tya = p.ve((tc, ya_tok[ot]), nc.vector.tensor_add,
                                   ya_h[:, ot * 512: ot * 512 + 512],
                                   ya_h[:, ot * 512: ot * 512 + 512], ctmp[:])
                        ya_tok[ot] = tya
                        p.tok[tag + "ctmp_war"] = tya
                    bcs[sl].freed = p.now("dve")
                    issue_bcast(n + 2)
                flush_state(0)
                gate_toks = []
                zt_tok = [None, None]

                def issue_z(ot):
                    if ot >= 8:
                        return
                    gi = half * 8 + ot
                    sl = ot % 2
                    zt_tok[sl] = p.dma("sp", zrd[sl][:],
                                       ap3(zst_t, gi * 128 * S + c0,
                                           [[S, 128], [1, 512]]),
                                       zr[sl],
                                       waits=(p.tok.get((tag + "zrd_war", sl)),
                                              p.tok.get((tag + "xcb_war", sl)),
                                              p.tok.get((tag + "zdram", gi // 4))))

                issue_z(0)
                issue_z(1)
                for ot in range(8):
                    sl = ot % 2
                    tsl = p.ae((zt_tok[sl], p.tok.get((tag + "stmp_war", sl))),
                               nc.scalar.activation, stmp2[sl][:], zrd[sl][:], AF.Silu)
                    p.tok[(tag + "zrd_war", sl)] = tsl
                    p.tok[(tag + "xcb_war", sl)] = tsl
                    issue_z(ot + 2)
                    tg = p.ve((tsl, ya_tok[ot], p.tok.get((tag + "ybf_war", ot))),
                              nc.vector.tensor_mul,
                              ybf[:, ot * 512: ot * 512 + 512],
                              ya_h[:, ot * 512: ot * 512 + 512], stmp2[sl][:])
                    p.tok[(tag + "stmp_war", sl)] = tg
                    p.tok[(tag + "ya_war", ot)] = tg
                    gate_toks.append(tg)
                for f in range(8):
                    wits = []
                    rws = []
                    for kt in range(8):
                        b_, c_, r_ = st_wo.get(f, half * 8 + kt)
                        wits.append((b_, c_))
                        rws.append(r_)
                    rws = tuple(set(rws))
                    bk = bank()
                    t = p.mm([(bk.t[:], wits[kt][0][:, wits[kt][1]:wits[kt][1] + 128],
                               ybf[:, kt * 512: kt * 512 + 512],
                               kt == 0, kt == 7) for kt in range(8)],
                             waits=tuple(gate_toks) + (bk.freed,) + rws)
                    bk.written = t
                    xsl = x_res[:, f * 512: f * 512 + 512]
                    vtok = p.ve((t,), nc.vector.tensor_add, xsl, bk.t[:], xsl)
                    bk.freed = vtok
                    pe_now = p.now("pe")
                    st_wo.done(f, pe_now)
                dve_now = p.now("dve")
                act_now = p.now("act")
                pool_now = p.now("pool")
                for ot in range(8):
                    p.tok[(tag + "u_war", ot)] = dve_now
                    p.tok[(tag + "ybf_war", ot)] = pe_now
                    p.tok[(tag + "dt_war", ot)] = (act_now, dve_now)
                    p.tok[(tag + "ya_war", ot)] = (p.tok.get((tag + "ya_war", ot)),
                                                   pool_now)
            for i in range(16):
                p.tok[(tag + "xc_war", i)] = (act_now, dve_now)
            p.tok[tag + "dtl_war"] = pe_now
            p.tok[tag + "xb_war"] = dve_now
            stok = p.dma("sp", ap3(dst_t, c0, [[S, 128], [128 * S, 8], [1, MTC]]),
                         x_res[:].rearrange("p (k t) -> p k t", k=8), xst,
                         waits=(vtok,))
            xld.freed = stok
    p.barrier()
    return stok
